# version 10
# speedup vs baseline: 2.2083x; 1.0487x over previous
import contextlib
import math

import numpy as np
import ml_dtypes
import concourse.bass as bass
import concourse.mybir as mybir
from concourse.ap import AP
from concourse.bass_utils import run_bass_kernel_spmd
from concourse.bass_core_rust import IndirectOffsetOnAxis

F32 = mybir.dt.float32
F32R = mybir.dt.float32r
BF16 = mybir.dt.bfloat16
I32 = mybir.dt.int32
U8 = mybir.dt.uint8
AF = mybir.ActivationFunctionType
ALU = mybir.AluOpType
AX = mybir.AxisListType

NCORE = 8
D = 1024
H = 16
DK = 64
E = 8
FF = 1024
THETA = 0.9
EPS = 1e-5
NEG = -1.0e9
CSH = 12.0
WWIN = 308
GW = 480
GOFF = 40
INT_MIN = -(2 ** 31)
M_NEG = int(np.int32(np.float32(NEG).view(np.int32)) ^ np.int32(-1))
DEBUG = False
SPARSE = True


def _bucket_table():
    d = np.arange(-217, 258)
    nb = H
    n = -d
    ret = (n < 0).astype(np.int32) * nb
    n_abs = np.abs(n)
    max_exact = nb // 2
    is_small = n_abs < max_exact
    n_safe = np.maximum(n_abs, 1).astype(np.float32)
    val = max_exact + (
        np.log(n_safe / np.float32(max_exact))
        / np.float32(math.log(128 / max_exact))
        * (nb - max_exact)
    ).astype(np.int32)
    val = np.minimum(val, nb - 1)
    return (ret + np.where(is_small, n_abs, val)).astype(np.int32)


class Emit:

    def __init__(self, sems, which, eng, nc):
        self.sems = sems
        self.cnt = {k: 0 for k in sems}
        self.which = which
        self.eng = eng
        self.nc = nc

    def val(self, sem):
        return self.cnt[sem]

    CHAIN = {"vector": "ve", "scalar": "se"}

    def do(self, name, fn, sem=None, n=1, chain=True):
        chain_s = self.CHAIN.get(name)
        before = self.cnt[chain_s] if chain_s else 0
        if chain_s and sem != chain_s:
            self.cnt[chain_s] += 1
        if sem is not None:
            self.cnt[sem] += n
        if self.which == name:
            ins = fn(self.eng)
            if ins is not None:
                if chain_s and chain and before > 0:
                    ins._wait_ge(self.sems[chain_s], before)
                if sem is not None:
                    ins.then_inc(self.sems[sem], n)
                if chain_s and sem != chain_s:
                    ins.then_inc(self.sems[chain_s], 1)
        return self.cnt[sem] if sem is not None else None

    def wait(self, name, sem, at=None):
        v = self.cnt[sem] if at is None else at
        if v <= 0:
            return
        if self.which == name:
            self.eng.wait_ge(self.sems[sem], v)

    def barrier(self):
        for s in self.sems:
            self.wait(self.which, s)


def _dt_size(dt_):
    return {F32: 4, F32R: 4, BF16: 2, I32: 4, U8: 1}[dt_]


def build_program(T, launch):
    N = NCORE * T
    NTI = T // 128
    NPL = N // 128
    QCH = min(512, T)
    NCH = T // QCH
    GCH = N // QCH
    CAP = N // E
    KTI = T // 128
    SLOTS = min(1024, CAP)
    SCH = min(512, SLOTS)
    NSC = SLOTS // SCH

    nc = bass.Bass("TRN2", target_bir_lowering=False, detect_race_conditions=False)

    if launch == "A":
        x_in = nc.dram_tensor("x0", [T, D], F32, kind="ExternalInput")
        x0T = nc.dram_tensor("x0T", [D, N], F32, kind="ExternalInput")
        wq = nc.dram_tensor("wq", [D, D], F32, kind="ExternalInput")
        wk = nc.dram_tensor("wk", [D, D], F32, kind="ExternalInput")
        wv = nc.dram_tensor("wv", [D, D], F32, kind="ExternalInput")
        wo = nc.dram_tensor("wo", [D, D], F32, kind="ExternalInput")
        rel = nc.dram_tensor("rel", [32, H], F32, kind="ExternalInput")
    else:
        x_in = nc.dram_tensor("x2", [T, D], F32, kind="ExternalInput")
        m3f = nc.dram_tensor("m3f", [128, NPL, E], I32, kind="ExternalInput")
        p3 = nc.dram_tensor("p3", [128, NTI, E], F32, kind="ExternalInput")
        x2nI = nc.dram_tensor("x2n", [T, D], BF16, kind="ExternalInput")
    gwo = nc.dram_tensor("gwo", [D, 16], F32, kind="ExternalInput")
    w1t = nc.dram_tensor("w1t", [E, D, FF], F32, kind="ExternalInput")
    w2t = nc.dram_tensor("w2t", [E, FF, D], F32, kind="ExternalInput")

    if launch == "A":
        x_out = nc.dram_tensor("x2o", [T, D], F32, kind="ExternalOutput")
        if DEBUG:
            dbg_pall = nc.dram_tensor("dbg_pall", [128, NPL, E], F32,
                                      kind="ExternalOutput")
            dbg_ssb = nc.dram_tensor("dbg_ssb", [128, NTI, E], F32,
                                     kind="ExternalOutput")
            dbg_raw = nc.dram_tensor("dbg_raw", [128, 10, NPL], BF16,
                                     kind="ExternalOutput")
            dbg_gb8 = nc.dram_tensor("dbg_gb8", [128, E], F32,
                                     kind="ExternalOutput")
            dbg_log = nc.dram_tensor("dbg_log", [128, NPL, E], F32,
                                     kind="ExternalOutput")
            dbg_m = nc.dram_tensor("dbg_m", [128, NPL], F32,
                                   kind="ExternalOutput")
            dbg_mall = nc.dram_tensor("dbg_mall", [128, NPL, E], I32,
                                      kind="ExternalOutput")
            dbg_thr = nc.dram_tensor("dbg_thr", [128, 4, E], F32,
                                     kind="ExternalOutput")
            dbg_sall = nc.dram_tensor("dbg_sall", [128, NPL, E], F32,
                                      kind="ExternalOutput")
            dbg_ctx = nc.dram_tensor("dbg_ctx", [H // 2, 128, T], BF16,
                                     kind="ExternalOutput")
            dbg_qt = nc.dram_tensor("dbg_qt", [128, T], BF16,
                                    kind="ExternalOutput")
            dbg_kt = nc.dram_tensor("dbg_kt", [128, T], BF16,
                                    kind="ExternalOutput")
            dbg_v = nc.dram_tensor("dbg_v", [128, KTI, 4, 72], BF16,
                                   kind="ExternalOutput")
            dbg_xnt2 = nc.dram_tensor("dbg_xnt2", [128, 8, T], BF16,
                                      kind="ExternalOutput")
            dbg_wtb = nc.dram_tensor("dbg_wtb", [128, 2, 2, QCH], BF16,
                                     kind="ExternalOutput")
            dbg_den = nc.dram_tensor("dbg_den", [1, QCH], F32,
                                     kind="ExternalOutput")
            dbg_rs = nc.dram_tensor("dbg_rs", [128, NPL], F32,
                                    kind="ExternalOutput")
        m3o = nc.dram_tensor("m3o", [128, NTI, E], I32, kind="ExternalOutput")
        p3o = nc.dram_tensor("p3o", [128, NTI, E], F32, kind="ExternalOutput")
        x2n = nc.dram_tensor("x2n", [T, D], BF16, kind="ExternalOutput")
        x1d = nc.dram_tensor("x1d", [T, D], F32,
                             kind="ExternalOutput" if DEBUG else "Internal")
        gtd = nc.dram_tensor("gtd", [H, GW], BF16, kind="Internal")
        ctxd = nc.dram_tensor("ctxd", [H // 2, 128, T], BF16, kind="Internal")
    else:
        x_out = nc.dram_tensor("x3o", [T, D], F32, kind="ExternalOutput")
    xnTd = nc.dram_tensor("xnTd", [128, 8, T], BF16, kind="Internal")
    if SPARSE:
        xgd = nc.dram_tensor("xgd", [E, SLOTS + 128, D], BF16, kind="Internal")
        ygd = nc.dram_tensor("ygd", [E, SLOTS, D], BF16, kind="Internal")
        crd = nc.dram_tensor("crd", [NTI, E], F32, kind="Internal")
    sald = nc.dram_tensor("sald", [128, NPL, E], F32, kind="Internal")

    ident_t = nc.inline_tensor(np.eye(128, dtype=ml_dtypes.bfloat16), "ident_t")
    ones1_t = nc.inline_tensor(np.ones((1, 128), dtype=ml_dtypes.bfloat16), "ones1_t")
    onesc_t = nc.inline_tensor(np.ones((128, 1), dtype=ml_dtypes.bfloat16), "onesc_t")
    onescf_t = nc.inline_tensor(np.ones((128, 1), dtype=np.float32), "onescf_t")
    idxg = np.arange(N, dtype=np.float32).reshape(NPL, 128).T.copy()
    idx_t = nc.inline_tensor(idxg, "idx_t")
    if SPARSE:
        ultri_np = np.triu(np.ones((128, 128), dtype=ml_dtypes.bfloat16), 1)
        ultri_t = nc.inline_tensor(ultri_np, "ultri_t")
        ultrif_t = nc.inline_tensor(
            np.triu(np.ones((128, 128), dtype=np.float32), 1), "ultrif_t")
        onesrf_t = nc.inline_tensor(np.ones((1, 128), dtype=np.float32),
                                    "onesrf_t")
    if launch == "A":
        buck = _bucket_table()
        oh = np.zeros((32, GW), dtype=ml_dtypes.bfloat16)
        for i in range(475):
            oh[buck[474 - i], i] = 1.0
        oh_t = nc.inline_tensor(oh, "oh_t")
        antid_t = nc.inline_tensor(
            np.eye(128, dtype=ml_dtypes.bfloat16)[::-1].copy(), "antid_t")

    with contextlib.ExitStack() as ctx:
        En = ctx.enter_context

        pall = En(nc.sbuf_tensor("pall", [128, NPL, E], F32))
        mall = En(nc.sbuf_tensor("mall", [128, NPL, E], I32))
        scrA = En(nc.sbuf_tensor("scrA", [128, NPL], F32))
        scrB = En(nc.sbuf_tensor("scrB", [128, NPL], F32))
        scrC = En(nc.sbuf_tensor("scrC", [128, NPL], F32))
        accS = En(nc.sbuf_tensor("accS", [128, NPL], F32))
        accR = En(nc.sbuf_tensor("accR", [128, NPL], F32))
        negp = En(nc.sbuf_tensor("negp", [128, NPL], F32))
        junk = En(nc.sbuf_tensor("junk", [128, NPL], F32))
        idxp = En(nc.sbuf_tensor("idxp", [128, NPL], F32))
        ssb = En(nc.sbuf_tensor("ssb", [128, NTI, E], F32))
        l3 = En(nc.sbuf_tensor("l3", [128, NTI, E], F32))
        xtile = En(nc.sbuf_tensor("xtile", [128, D], F32))
        xtile2 = En(nc.sbuf_tensor("xtile2", [128, D], F32))
        sq = En(nc.sbuf_tensor("sq", [128, D], F32))
        xnb = En(nc.sbuf_tensor("xnb", [128, D], BF16))
        xnstage = En(nc.sbuf_tensor("xnstage", [128, 8, 128], BF16))
        stt = En(nc.sbuf_tensor("stt", [128, 12], F32))
        idb = En(nc.sbuf_tensor("idb", [128, 128], BF16))
        if SPARSE:
            ultrib = En(nc.sbuf_tensor("ultrib", [128, 128], BF16))
            ultrif = En(nc.sbuf_tensor("ultrif", [128, 128], F32))
            onesrf = En(nc.sbuf_tensor("onesrf", [1, 128], F32))
        onesrb = En(nc.sbuf_tensor("onesrb", [1, 128], BF16))
        onescb = En(nc.sbuf_tensor("onescb", [128, 1], BF16))
        onescf = En(nc.sbuf_tensor("onescf", [128, 1], F32))
        gwob = En(nc.sbuf_tensor("gwob", [128, 8, 16], BF16))
        bhi = En(nc.sbuf_tensor("bhi", [128, E], F32))
        blo = En(nc.sbuf_tensor("blo", [128, E], F32))
        ibs = En(nc.sbuf_tensor("ibs", [128, E], F32))
        selsh = En(nc.sbuf_tensor("selsh", [128, E], F32))
        mstr = En(nc.sbuf_tensor("mstr", [128, E], F32))
        cnt8 = En(nc.sbuf_tensor("cnt8", [128, E], F32))
        cnt8b = En(nc.sbuf_tensor("cnt8b", [128, E], BF16))
        sel8 = En(nc.sbuf_tensor("sel8", [1, E], F32))
        sel8b = En(nc.sbuf_tensor("sel8b", [1, E], BF16))
        need8 = En(nc.sbuf_tensor("need8", [128, E], F32))
        cshcol = En(nc.sbuf_tensor("cshcol", [128, 1], F32))
        if launch == "A":
            gb8 = En(nc.sbuf_tensor("gb8", [128, E], F32))
            relb = En(nc.sbuf_tensor("relb", [32, H], BF16))
            cv32 = En(nc.sbuf_tensor("cv32", [1, 32], BF16))
            cbias = En(nc.sbuf_tensor("cbias", [128, 32], F32))
            cbcols = [En(nc.sbuf_tensor(f"cbc{i}", [128, 1], F32))
                      for i in range(32)]
            bband = En(nc.sbuf_tensor("bband", [128, H, 320], BF16))
            antib = En(nc.sbuf_tensor("antib", [128, 128], BF16))
            ohb = En(nc.sbuf_tensor("ohb", [32, GW], BF16))
            g19 = En(nc.sbuf_tensor("g19", [1, 16], F32))
            g19b = En(nc.sbuf_tensor("g19b", [1, 16], BF16))

        def arena(name, nbytes):
            return En(nc.sbuf_tensor(name, [128, nbytes], U8))

        AR1 = arena("AR1", 65536)
        AR2 = arena("AR2", 16384)
        AR3 = arena("AR3", 18432)
        AR4 = arena("AR4", 8192)
        AR5 = arena("AR5", 8192)
        AR6 = arena("AR6", 8192)
        AR7 = arena("AR7", 8192)
        AR8 = arena("AR8", 4096)
        AR9 = arena("AR9", 4096)
        AR10 = arena("AR10", 2048)
        AR11 = arena("AR11", 2048)

        def carve(ar, off, shape, dt_, parts=128):
            nby = int(np.prod(shape)) * _dt_size(dt_)
            v = ar[0:parts, off:off + nby].bitcast(dt_)
            if len(shape) == 1:
                return v
            s = "p (" + " ".join(f"a{i}" for i in range(len(shape))) + ") -> p " + \
                " ".join(f"a{i}" for i in range(len(shape)))
            kw = {f"a{i}": shape[i] for i in range(len(shape))}
            return v.rearrange(s, **kw)

        yacc = carve(AR1, 0, [NTI, D], BF16)
        mhi = carve(AR2, 0, [NPL, E], F32)
        mlo = carve(AR2, NPL * E * 4, [NPL, E], F32)
        if SPARSE:
            xnall = carve(AR1, 0, [NTI, D], BF16)
            xslab = carve(AR7, 0, [4, D], BF16)
            xsT = carve(AR4, 0, [8, SCH], BF16)
            ysb = carve(AR6, 0, [2, D], BF16)
            gbuf = carve(AR4, 0, [4, D], BF16)
            sidx = carve(AR8, 0, [E, NTI], I32)
            gidx = carve(AR8, NTI * E * 4, [E, NTI], I32)
            colsumb = carve(AR10, 0, [E], BF16, parts=NTI)
            colpf = carve(AR10, 1536, [E], F32, parts=NTI)
            kb2 = carve(AR9, 0, [E, NTI], BF16)
            crow = AR10[0:1, 512:512 + NTI * E * 4].bitcast(F32)
            zbp = carve(AR11, 0, [D], BF16)
        w1b = carve(AR2, 0, [8, FF], BF16)
        w2b = carve(AR3, 0, [8, D], BF16)
        xnTc = carve(AR4, 0, [8, QCH], BF16)
        hTc = carve(AR5, 0, [8, QCH], BF16)
        eqh_pl = carve(AR5, 0, [NPL, E], F32)
        p1s2 = [carve(AR2, 0, [8, QCH], F32), carve(AR3, 0, [8, QCH], F32)]
        p1b2 = [carve(AR4, 0, [8, QCH], BF16), carve(AR5, 0, [8, QCH], BF16)]
        p1sqb2 = carve(AR10, 0, [2, QCH], BF16)
        gtp = carve(AR8, 0, [NPL, E], BF16)
        eqp = carve(AR9, 0, [NPL, E], BF16)
        sall = carve(AR6, 0, [NPL, E], F32)
        xnTcB = carve(AR7, 0, [8, QCH], BF16)
        rawT = carve(AR7, 0, [12, NPL], BF16)
        st10_2 = [AR8[0:33, 0:QCH * 2].bitcast(BF16),
                  AR8[0:33, QCH * 2:QCH * 4].bitcast(BF16)]
        if launch == "A":
            xnT2 = carve(AR1, 0, [8, T], BF16)
            qt2 = carve(AR4, 0, [T], BF16)
            kt2s = carve(AR5, 0, [T], BF16)
            vsb = carve(AR3, 0, [KTI, 4, 72], BF16)
            wtb = carve(AR8, 0, [2, 2, QCH], BF16)
            ctxp = carve(AR6, 0, [T], BF16)
            ctxc = carve(AR7, 0, [8, QCH], BF16)
            wqb = carve(AR10, 0, [8, 128], BF16)
            wkb = carve(AR11, 0, [8, 128], BF16)
            wvb = carve(AR9, 0, [8, 256], BF16)
            wob = carve(AR2, 0, [8, D], BF16)
            den_h = [AR2[0:1, 0:QCH * 4].bitcast(F32),
                     AR2[0:1, QCH * 4:QCH * 8].bitcast(F32)]
            denb_h = [AR2[0:1, QCH * 8:QCH * 10].bitcast(BF16),
                      AR2[0:1, QCH * 10:QCH * 12].bitcast(BF16)]
            ctr_h = [AR2[0:64, 8192:8192 + QCH * 2].bitcast(BF16),
                     AR2[0:64, 8192 + QCH * 2:8192 + QCH * 4].bitcast(BF16)]
            den = den_h[0]
            hh = AR4[0:128, 0:WWIN * 2].bitcast(BF16)
            grow = AR5[0:16, 0:GW * 2].bitcast(BF16)

        pb = [En(nc.psum_tensor(f"pb{i}", [128, 512], F32)) for i in range(8)]
        sems = {k: En(nc.semaphore(k)) for k in ["ld", "lg", "mm", "se", "ve", "st"]}
        block = En(nc.Block())

        def tview(bank):
            return pb[bank][0:128, 0:64].bitcast(BF16)

        def ln_tile(em, src_dram, t, out=None):
            ld_cp = em.do("sync", lambda g: g.dma_start(
                out=xtile[:, :], in_=src_dram.ap()[t * 128:(t + 1) * 128, :]),
                "ld", 16)
            em.wait("vector", "ld", ld_cp)
            em.do("vector", lambda g: g.reduce_sum(stt[:, 0:1], xtile[:, :], AX.X),
                  "ve")
            em.wait("scalar", "ld", ld_cp)
            em.do("scalar", lambda g: g.activation(
                sq[:, :], xtile[:, :], AF.Square, accum_out=stt[:, 1:2]), "se")
            se_a = em.val("se")
            em.wait("vector", "se", se_a)
            em.do("vector", lambda g: g.tensor_scalar_mul(
                stt[:, 2:3], stt[:, 0:1], 1.0 / D), "ve")
            em.do("vector", lambda g: g.tensor_scalar_mul(
                stt[:, 3:4], stt[:, 1:2], 1.0 / D), "ve")
            em.do("vector", lambda g: g.tensor_mul(
                stt[:, 4:5], stt[:, 2:3], stt[:, 2:3]), "ve")
            em.do("vector", lambda g: g.tensor_sub(
                stt[:, 3:4], stt[:, 3:4], stt[:, 4:5]), "ve")
            em.do("vector", lambda g: g.tensor_scalar_add(
                stt[:, 3:4], stt[:, 3:4], float(EPS)), "ve")
            ve_b = em.val("ve")
            em.wait("scalar", "ve", ve_b)
            em.do("scalar", lambda g: g.activation(
                stt[:, 5:6], stt[:, 3:4], AF.Ln), "se")
            em.do("scalar", lambda g: g.activation(
                stt[:, 6:7], stt[:, 5:6], AF.Exp, scale=-0.5), "se")
            se_b = em.val("se")
            em.wait("vector", "se", se_b)
            dst = xnb[:, :] if out is None else out
            em.do("vector", lambda g, d4=dst: g.tensor_scalar(
                d4, xtile[:, :], stt[:, 2:3], stt[:, 6:7],
                ALU.subtract, ALU.mult), "ve")
            return em.val("ve"), se_b, ld_cp

        def transpose_xnb(em, ve_c):
            for dt in range(8):
                em.wait("tensor", "ve", ve_c if dt == 0 else em.val("ve"))
                em.do("tensor", lambda g, d2=dt: g.transpose(
                    tview(3), xnb[:, d2 * 128:(d2 + 1) * 128], idb[:, :]), "mm")
                em.wait("vector", "mm")
                em.do("vector", lambda g, d2=dt: g.tensor_copy(
                    xnstage[:, d2, :], tview(3)), "ve", chain=False)
            return em.val("ve")

        def ln_spill_stepper(em, src_dram):
            state = {"t": 0, "prev": [0, 0, 0]}

            def step():
                t = state["t"]
                if t >= NTI:
                    return False
                prev = state["prev"]
                em.wait("sync", "ve", prev[0])
                em.wait("sync", "se", prev[1])
                if SPARSE:
                    ve_c, se_b, _ = ln_tile(em, src_dram, t,
                                            out=xnall[:, t, :])
                    state["prev"] = [ve_c, se_b, 0]
                else:
                    em.wait("vector", "st", prev[2])
                    ve_c, se_b, _ = ln_tile(em, src_dram, t)
                    ve_d = transpose_xnb(em, ve_c)
                    em.wait("sync", "ve", ve_d)
                    st_cp = em.do("sync", lambda g, tt=t: g.dma_start(
                        out=xnTd.ap()[:, :, tt * 128:(tt + 1) * 128],
                        in_=xnstage[:, :, :]), "st", 16)
                    state["prev"] = [ve_d, se_b, st_cp]
                state["t"] = t + 1
                return True

            return step

        def ffn(em, x_src, x_dst):
            em.do("vector", lambda g: g.memset(
                yacc.rearrange("p a b -> p (a b)"), 0.0), "ve")
            lg_w1 = {}
            lg_w2 = {}
            mm_h_last = {-1: 0}
            mm_e_last = {-1: 0}
            ld_x = {}
            x_done = {}
            yv_cp = {}

            def load_w1(e_):
                em.wait("gpsimd", "mm", mm_h_last[e_ - 1])
                lg_w1[e_] = em.do("gpsimd", lambda g, e2=e_: g.dma_start(
                    out=w1b, in_=w1t.ap()[e2].rearrange("(dt p) f -> p dt f",
                                                        p=128)), "lg", 16)

            def load_w2(e_):
                em.wait("gpsimd", "mm", mm_e_last[e_ - 1])
                lg_w2[e_] = em.do("gpsimd", lambda g, e2=e_: g.dma_start(
                    out=w2b, in_=w2t.ap()[e2].rearrange("(ft p) d -> p ft d",
                                                        p=128)), "lg", 16)

            def load_x(idx):
                ch = idx % NCH
                bi = idx % 2
                em.wait("sync", "mm", x_done.get(idx - 2, 0))
                ld_x[idx] = em.do("sync", lambda g, c2=ch, b3=bi: g.dma_start(
                    out=(xnTc if b3 == 0 else xnTcB),
                    in_=xnTd.ap()[:, :, c2 * QCH:(c2 + 1) * QCH]), "ld", 16)

            load_w1(0)
            load_w2(0)
            load_x(0)
            for e_ in range(E):
                for ch in range(NCH):
                    idx = e_ * NCH + ch
                    xb_ = xnTc if idx % 2 == 0 else xnTcB
                    if idx + 1 < E * NCH:
                        load_x(idx + 1)
                    em.wait("tensor", "ld", ld_x[idx])
                    if ch == 0:
                        em.wait("tensor", "lg", lg_w1[e_])
                    gelu_cps = {}
                    for ft in range(8):
                        bank = ft % 2
                        if ft >= 2:
                            em.wait("tensor", "se", gelu_cps[ft - 2])
                        for dt in range(8):
                            em.do("tensor", lambda g, f2=ft, d2=dt, b2=bank,
                                  x4=xb_: g.matmul(
                                      pb[b2][0:128, 0:QCH],
                                      w1b[:, d2, f2 * 128:(f2 + 1) * 128],
                                      x4[:, d2, :], start=(d2 == 0),
                                      stop=(d2 == 7)), "mm")
                        mm_cp = em.val("mm")
                        em.wait("scalar", "mm", mm_cp)
                        em.do("scalar", lambda g, f2=ft, b2=bank: g.activation(
                            hTc[:, f2, :], pb[b2][0:128, 0:QCH], AF.Gelu), "se",
                            chain=False)
                        gelu_cps[ft] = em.val("se")
                    x_done[idx] = em.val("mm")
                    if ch == NCH - 1:
                        mm_h_last[e_] = em.val("mm")
                        if e_ + 1 < E:
                            load_w1(e_ + 1)
                    se_h = em.val("se")
                    if ch == 0:
                        em.wait("tensor", "lg", lg_w2[e_])
                    for st_ in range(QCH // 128):
                        tglob = ch * (QCH // 128) + st_
                        for dh in range(2):
                            bank = 2 + ((st_ * 2 + dh) % 2)
                            em.wait("tensor", "se", se_h)
                            em.wait("tensor", "ve", yv_cp.get(bank, 0))
                            for ft in range(8):
                                em.do("tensor", lambda g, f2=ft, s2=st_, d2=dh,
                                      b2=bank: g.matmul(
                                          pb[b2][0:128, 0:512],
                                          hTc[:, f2, s2 * 128:(s2 + 1) * 128],
                                          w2b[:, f2, d2 * 512:(d2 + 1) * 512],
                                          start=(f2 == 0), stop=(f2 == 7)), "mm")
                            mm_cp = em.val("mm")
                            em.wait("vector", "mm", mm_cp)
                            em.do("vector", lambda g, t2=tglob, d2=dh, b2=bank,
                                  e2=e_: g.scalar_tensor_tensor(
                                      yacc[:, t2, d2 * 512:(d2 + 1) * 512],
                                      pb[b2][0:128, 0:512],
                                      ssb[:, t2, e2:e2 + 1],
                                      yacc[:, t2, d2 * 512:(d2 + 1) * 512],
                                      ALU.mult, ALU.add), "ve", chain=False)
                            yv_cp[bank] = em.val("ve")
                mm_e_last[e_] = em.val("mm")
                if e_ + 1 < E:
                    load_w2(e_ + 1)
            ve_y = em.val("ve")
            prev = [0, 0]
            for t in range(NTI):
                em.wait("sync", "ve", prev[0])
                em.wait("sync", "st", prev[1])
                ld_cp = em.do("sync", lambda g, tt=t: g.dma_start(
                    out=xtile2[:, :], in_=x_src.ap()[tt * 128:(tt + 1) * 128, :]),
                    "ld", 16)
                em.wait("vector", "ld", ld_cp)
                em.wait("vector", "ve", ve_y)
                em.wait("vector", "st", prev[1])
                em.do("vector", lambda g, tt=t: g.scalar_tensor_tensor(
                    xtile[:, :], yacc[:, tt, :], 0.5, xtile2[:, :],
                    ALU.mult, ALU.add), "ve")
                ve_cp = em.val("ve")
                em.wait("sync", "ve", ve_cp)
                st_cp = em.do("sync", lambda g, tt=t: g.dma_start(
                    out=x_dst.ap()[tt * 128:(tt + 1) * 128, :], in_=xtile[:, :]),
                    "st", 16)
                prev = [ve_cp, st_cp]

        def pos_build(em):
            for e_ in range(E):
                em.do("vector", lambda g, e2=e_: g.tensor_single_scalar(
                    kb2[:, e2, :], ssb[:, :, e2], 0.0, ALU.is_gt), "ve",
                    chain=False)
            em.do("vector", lambda g: g.tensor_scalar_mul(
                scrB[:, 0:NPL], ssb[:, :, :].rearrange("p a b -> p (a b)"),
                0.5), "ve")
            ve_kb = em.val("ve")
            em.wait("tensor", "ve", ve_kb)
            for e_ in range(E):
                em.do("tensor", lambda g, e2=e_: g.matmul(
                    pb[2][0:128, e2 * NTI:(e2 + 1) * NTI], ultrib[:, :],
                    kb2[:, e2, :], start=True, stop=True), "mm")
                em.do("tensor", lambda g, e2=e_: g.matmul(
                    pb[3][0:NTI, e2:e2 + 1], kb2[:, e2, :], onescb[:, :],
                    start=True, stop=True), "mm")
            mm_cp = em.val("mm")
            em.wait("vector", "mm", mm_cp)
            em.do("vector", lambda g: g.tensor_copy(
                colsumb[0:NTI, :], pb[3][0:NTI, 0:E]), "ve")
            ve_cp = em.val("ve")
            em.wait("tensor", "ve", ve_cp)
            em.do("tensor", lambda g: g.matmul(
                pb[3][0:NTI, 0:E], ultrib[0:NTI, 0:NTI], colsumb[0:NTI, :],
                start=True, stop=True), "mm")
            mm_cp = em.val("mm")
            em.wait("vector", "mm", mm_cp)
            em.do("vector", lambda g: g.tensor_copy(
                colpf[0:NTI, :], pb[3][0:NTI, 0:E]), "ve")
            ve_cp = em.val("ve")
            em.wait("sync", "ve", ve_cp)
            st_cp = em.do("sync", lambda g: g.dma_start(
                out=crd.ap(), in_=colpf[0:NTI, :]), "st", 16)
            em.wait("sync", "st", st_cp)
            ld_cp = em.do("sync", lambda g: g.dma_start(
                out=crow[0:1, :].rearrange("p (e t) -> p e t", e=E),
                in_=crd.ap().rearrange("t e -> () e t"),
                allow_slow_non_contiguous=True), "ld", 16)
            em.wait("tensor", "ld", ld_cp)
            em.do("tensor", lambda g: g.matmul(
                pb[3][0:128, 0:NTI * E], onesrf[:, :], crow[0:1, :],
                start=True, stop=True), "mm")
            mm_cp = em.val("mm")
            em.wait("vector", "mm", mm_cp)
            em.do("vector", lambda g: g.tensor_copy(
                scrA[:, 0:NPL], pb[2][0:128, 0:NTI * E]), "ve")
            em.do("vector", lambda g: g.tensor_add(
                scrA[:, 0:NPL], scrA[:, 0:NPL], pb[3][0:128, 0:NTI * E]), "ve")
            em.do("vector", lambda g: g.tensor_copy(
                junk[:, 0:NPL], kb2[:, :, :].rearrange("p a b -> p (a b)")),
                "ve")
            em.do("vector", lambda g: g.tensor_single_scalar(
                scrC[:, 0:NPL], scrA[:, 0:NPL], float(SLOTS), ALU.is_lt), "ve")
            em.do("vector", lambda g: g.tensor_mul(
                scrC[:, 0:NPL], scrC[:, 0:NPL], junk[:, 0:NPL]), "ve")
            em.do("vector", lambda g: g.tensor_scalar_add(
                accS[:, 0:NPL], scrA[:, 0:NPL], float(-SLOTS)), "ve")
            em.do("vector", lambda g: g.tensor_mul(
                accS[:, 0:NPL], accS[:, 0:NPL], scrC[:, 0:NPL]), "ve")
            em.do("vector", lambda g: g.tensor_scalar_add(
                accS[:, 0:NPL], accS[:, 0:NPL], float(SLOTS) + 8388608.0),
                "ve")
            em.do("vector", lambda g: g.tensor_single_scalar(
                sidx[:, :, :].rearrange("p a b -> p (a b)"),
                accS[:, 0:NPL].bitcast(I32), 0x007FFFFF, ALU.bitwise_and),
                "ve")
            em.do("vector", lambda g: g.tensor_single_scalar(
                accR[:, 0:NPL], scrA[:, 0:NPL], float(SLOTS - 1), ALU.min),
                "ve")
            em.do("vector", lambda g: g.tensor_scalar_add(
                accR[:, 0:NPL], accR[:, 0:NPL], 8388608.0), "ve")
            em.do("vector", lambda g: g.tensor_single_scalar(
                gidx[:, :, :].rearrange("p a b -> p (a b)"),
                accR[:, 0:NPL].bitcast(I32), 0x007FFFFF, ALU.bitwise_and),
                "ve")

        def ln_sink_tile(em, t, x_ve, prev_tp_mm):
            em.do("vector", lambda g: g.reduce_sum(stt[:, 0:1], xtile[:, :], AX.X),
                  "ve")
            em.wait("scalar", "ve", x_ve)
            em.do("scalar", lambda g: g.activation(
                sq[:, :], xtile[:, :], AF.Square, accum_out=stt[:, 1:2]), "se")
            sq_se = em.val("se")
            em.wait("vector", "se", sq_se)
            em.do("vector", lambda g: g.tensor_scalar_mul(
                stt[:, 2:3], stt[:, 0:1], 1.0 / D), "ve")
            em.do("vector", lambda g: g.tensor_scalar_mul(
                stt[:, 3:4], stt[:, 1:2], 1.0 / D), "ve")
            em.do("vector", lambda g: g.tensor_mul(
                stt[:, 4:5], stt[:, 2:3], stt[:, 2:3]), "ve")
            em.do("vector", lambda g: g.tensor_sub(
                stt[:, 3:4], stt[:, 3:4], stt[:, 4:5]), "ve")
            em.do("vector", lambda g: g.tensor_scalar_add(
                stt[:, 3:4], stt[:, 3:4], float(EPS)), "ve")
            ve_b = em.val("ve")
            em.wait("scalar", "ve", ve_b)
            em.do("scalar", lambda g: g.activation(
                stt[:, 5:6], stt[:, 3:4], AF.Ln), "se")
            em.do("scalar", lambda g: g.activation(
                stt[:, 6:7], stt[:, 5:6], AF.Exp, scale=-0.5), "se")
            se_b = em.val("se")
            em.wait("vector", "se", se_b)
            em.wait("vector", "mm", prev_tp_mm)
            em.do("vector", lambda g: g.tensor_scalar(
                xnb[:, :], xtile[:, :], stt[:, 2:3], stt[:, 6:7],
                ALU.subtract, ALU.mult), "ve")
            ve_c = em.val("ve")
            for dt in range(8):
                em.wait("tensor", "ve", ve_c if dt == 0 else em.val("ve"))
                em.do("tensor", lambda g, d2=dt: g.transpose(
                    tview(3), xnb[:, d2 * 128:(d2 + 1) * 128], idb[:, :]), "mm")
                em.wait("vector", "mm")
                em.do("vector", lambda g, d2=dt, tt=t: g.tensor_copy(
                    xnT2[:, d2, tt * 128:(tt + 1) * 128], tview(3)), "ve",
                    chain=False)
            return sq_se, em.val("mm")

        def ffn_sparse(em, x_src, x_dst, zf_st, ln_sink=False):
            pos_build(em)
            pos_ve = em.val("ve")
            NB = SCH // 128
            lg_w1 = {}
            lg_w2 = {}
            sc_lg = {}
            yst = {}
            mm_h_last = {-1: 0}
            mm_e_last = {-1: 0}

            def load_w1(e_):
                em.wait("gpsimd", "mm", mm_h_last[e_ - 1])
                lg_w1[e_] = em.do("gpsimd", lambda g, e2=e_: g.dma_start(
                    out=w1b, in_=w1t.ap()[e2].rearrange("(dt p) f -> p dt f",
                                                        p=128)), "lg", 16)

            def load_w2(e_):
                em.wait("gpsimd", "mm", mm_e_last[e_ - 1])
                lg_w2[e_] = em.do("gpsimd", lambda g, e2=e_: g.dma_start(
                    out=w2b, in_=w2t.ap()[e2].rearrange("(ft p) d -> p ft d",
                                                        p=128)), "lg", 16)

            def scatter(e_):
                if e_ == 0:
                    em.wait("gpsimd", "ve", pos_ve)
                    em.wait("gpsimd", "st", zf_st)
                for t in range(NTI):
                    em.do("gpsimd", lambda g, e2=e_, t2=t: g.indirect_dma_start(
                        out=xgd.ap().rearrange("e r d -> (e r) d"),
                        out_offset=IndirectOffsetOnAxis(sidx[:, e2, t2:t2 + 1],
                                                        0),
                        in_=xnall[:, t2, :], in_offset=None,
                        element_offset=e2 * (SLOTS + 128) * D), "lg", 16)
                sc_lg[e_] = em.val("lg")

            scatter(0)
            load_w1(0)
            load_w2(0)
            tp_done = 0
            st_of = {}
            yevac_prev = [0]
            for e_ in range(E):
                for sc in range(NSC):
                    em.wait("sync", "lg", sc_lg[e_])
                    em.wait("sync", "mm", tp_done)
                    ld_x = em.do("sync", lambda g, e2=e_, s2=sc: g.dma_start(
                        out=xslab[:, 0:NB, :],
                        in_=xgd.ap()[e2][s2 * SCH:(s2 + 1) * SCH, :]
                            .rearrange("(b p) d -> p b d", p=128)), "ld", 16)
                    tev = {}
                    em.wait("tensor", "ld", ld_x)
                    for i_ in range(NB * 8):
                        dt, b = divmod(i_, NB)
                        bk = 6 + (i_ % 2)
                        em.wait("tensor", "ve", tev.get(i_ - 2, 0))
                        em.do("tensor", lambda g, b2=b, d2=dt, k2=bk:
                              g.transpose(
                                  tview(k2),
                                  xslab[:, b2, d2 * 128:(d2 + 1) * 128],
                                  idb[:, :]), "mm")
                        em.wait("vector", "mm")
                        em.do("vector", lambda g, b2=b, d2=dt, k2=bk:
                              g.tensor_copy(
                                  xsT[:, d2, b2 * 128:(b2 + 1) * 128],
                                  tview(k2)), "ve", chain=False)
                        tev[i_] = em.val("ve")
                    tp_done = em.val("mm")
                    xsT_ve = em.val("ve")
                    em.wait("tensor", "ve", xsT_ve)
                    if sc == 0:
                        em.wait("tensor", "lg", lg_w1[e_])
                    gelu_cps = {}
                    for ft in range(8):
                        bank = ft % 2
                        if ft >= 2:
                            em.wait("tensor", "se", gelu_cps[ft - 2])
                        for dt in range(8):
                            em.do("tensor", lambda g, f2=ft, d2=dt, b2=bank:
                                  g.matmul(
                                      pb[b2][0:128, 0:SCH],
                                      w1b[:, d2, f2 * 128:(f2 + 1) * 128],
                                      xsT[:, d2, :], start=(d2 == 0),
                                      stop=(d2 == 7)), "mm")
                        mm_cp = em.val("mm")
                        em.wait("scalar", "mm", mm_cp)
                        em.do("scalar", lambda g, f2=ft, b2=bank: g.activation(
                            hTc[:, f2, 0:SCH], pb[b2][0:128, 0:SCH], AF.Gelu),
                            "se", chain=False)
                        gelu_cps[ft] = em.val("se")
                    if sc == NSC - 1:
                        mm_h_last[e_] = em.val("mm")
                        if e_ + 1 < E:
                            scatter(e_ + 1)
                            load_w1(e_ + 1)
                    se_h = em.val("se")
                    if sc == 0:
                        em.wait("tensor", "lg", lg_w2[e_])
                    for st_ in range(NB):
                        yb = st_ % 2
                        em.wait("vector", "st", st_of.get(yb, 0))
                        for dh in range(2):
                            bank = 2 + dh
                            em.wait("tensor", "se", se_h)
                            em.wait("tensor", "ve", yevac_prev[0])
                            for ft in range(8):
                                em.do("tensor", lambda g, f2=ft, s2=st_, d2=dh,
                                      b2=bank: g.matmul(
                                          pb[b2][0:128, 0:512],
                                          hTc[:, f2, s2 * 128:(s2 + 1) * 128],
                                          w2b[:, f2, d2 * 512:(d2 + 1) * 512],
                                          start=(f2 == 0), stop=(f2 == 7)),
                                      "mm")
                            mm_cp = em.val("mm")
                            em.wait("vector", "mm", mm_cp)
                            em.do("vector", lambda g, d2=dh, b2=bank, y2=yb:
                                  g.tensor_copy(
                                      ysb[:, y2, d2 * 512:(d2 + 1) * 512],
                                      pb[b2][0:128, 0:512]), "ve",
                                  chain=False)
                        yevac_prev[0] = em.val("ve")
                        ve_cp = em.val("ve")
                        em.wait("sync", "ve", ve_cp)
                        st_of[yb] = em.do("sync",
                              lambda g, e2=e_, s2=sc, t2=st_, y2=yb:
                              g.dma_start(
                                  out=ygd.ap()[e2][s2 * SCH + t2 * 128:
                                                   s2 * SCH + (t2 + 1) * 128,
                                                   :],
                                  in_=ysb[:, y2, :]), "st", 16)
                mm_e_last[e_] = em.val("mm")
                yst[e_] = em.val("st")
                if e_ + 1 < E:
                    load_w2(e_ + 1)
            prev_ve = 0
            prev_st = 0
            prev_sq_se = 0
            prev_tp_mm = 0
            gadd_ve = {}
            gath_cp = {}
            for t in range(NTI):
                em.wait("sync", "ve", prev_ve)
                em.wait("sync", "st", prev_st)
                ld_x2 = em.do("sync", lambda g, t2=t: g.dma_start(
                    out=xtile2[:, :],
                    in_=x_src.ap()[t2 * 128:(t2 + 1) * 128, :]), "ld", 16)
                for e_ in range(E):
                    gb2 = e_ % 4
                    if t == 0 and e_ == 0:
                        em.wait("gpsimd", "st", yst[E - 1])
                    em.wait("gpsimd", "ve", gadd_ve.get((t, e_ - 4), 0)
                            if e_ >= 4 else gadd_ve.get((t - 1, E - 4 + e_),
                                                        0))
                    gath_cp[(t, e_)] = em.do(
                        "gpsimd", lambda g, e2=e_, t2=t, g2=gb2:
                        g.indirect_dma_start(
                            out=gbuf[:, g2, :], out_offset=None,
                            in_=ygd.ap().rearrange("e r d -> (e r) d"),
                            in_offset=IndirectOffsetOnAxis(
                                gidx[:, e2, t2:t2 + 1], 0),
                            element_offset=e2 * SLOTS * D), "lg", 16)
                    em.wait("vector", "lg", gath_cp[(t, e_)])
                    if e_ == 0:
                        em.wait("vector", "ld", ld_x2)
                        em.wait("vector", "st", prev_st)
                        em.wait("vector", "se", prev_sq_se)
                    em.do("vector", lambda g, e2=e_, t2=t, g2=gb2:
                          g.scalar_tensor_tensor(
                              xtile[:, :], gbuf[:, g2, :],
                              scrB[:, (t2 * E + e2):(t2 * E + e2 + 1)],
                              xtile2[:, :] if e2 == 0 else xtile[:, :],
                              ALU.mult, ALU.add), "ve")
                    gadd_ve[(t, e_)] = em.val("ve")
                ve_cp = em.val("ve")
                prev_ve = ve_cp
                em.wait("sync", "ve", ve_cp)
                prev_st = em.do("sync", lambda g, t2=t: g.dma_start(
                    out=x_dst.ap()[t2 * 128:(t2 + 1) * 128, :],
                    in_=xtile[:, :]), "st", 16)
                if ln_sink:
                    prev_sq_se, prev_tp_mm = ln_sink_tile(em, t, ve_cp,
                                                          prev_tp_mm)

        def probs_inplace(em, buf, nsl):
            em.wait("scalar", "ve")
            em.do("scalar", lambda g: g.activation(
                buf[:, 0:nsl, :], buf[:, 0:nsl, :], AF.Exp), "se")
            em.wait("vector", "se")
            em.do("vector", lambda g: g.tensor_reduce(
                scrB[:, 0:nsl], buf[:, 0:nsl, :], AX.X, ALU.add), "ve")
            em.do("vector", lambda g: g.reciprocal(scrB[:, 0:nsl], scrB[:, 0:nsl]),
                  "ve")
            for e_ in range(E):
                em.do("vector", lambda g, e2=e_: g.tensor_mul(
                    buf[:, 0:nsl, e2], buf[:, 0:nsl, e2], scrB[:, 0:nsl]), "ve")

        def routing_math(em, pbuf, mbuf, nsl):
            def p_of(e_):
                return pbuf[:, 0:nsl, e_]

            for e_ in range(E):
                em.do("vector", lambda g: g.memset(accS[:, 0:nsl], 0.0), "ve")
                em.do("vector", lambda g: g.memset(accR[:, 0:nsl], 0.0), "ve")
                for i_ in range(E):
                    if i_ == e_:
                        continue
                    op = ALU.is_ge if i_ < e_ else ALU.is_gt
                    em.do("vector", lambda g, i2=i_, o2=op: g.tensor_tensor(
                        scrA[:, 0:nsl], p_of(i2), p_of(e_), o2), "ve")
                    em.do("vector", lambda g, i2=i_: g.tensor_mul(
                        scrC[:, 0:nsl], scrA[:, 0:nsl], p_of(i2)), "ve")
                    em.do("vector", lambda g: g.tensor_add(
                        accS[:, 0:nsl], accS[:, 0:nsl], scrC[:, 0:nsl]), "ve")
                    em.do("vector", lambda g: g.tensor_add(
                        accR[:, 0:nsl], accR[:, 0:nsl], scrA[:, 0:nsl]), "ve")
                em.do("vector", lambda g: g.tensor_sub(
                    scrA[:, 0:nsl], p_of(e_), accR[:, 0:nsl]), "ve")
                em.do("vector", lambda g: g.tensor_scalar_add(
                    scrA[:, 0:nsl], scrA[:, 0:nsl], -1.0), "ve")
                em.do("vector", lambda g: g.tensor_single_scalar(
                    scrC[:, 0:nsl].bitcast(I32), accS[:, 0:nsl], THETA,
                    ALU.is_lt), "ve")
                em.do("vector", lambda g: g.select(
                    accR[:, 0:nsl], scrC[:, 0:nsl].bitcast(I32),
                    scrA[:, 0:nsl], negp[:, 0:nsl]), "ve")
                vi = accR[:, 0:nsl].bitcast(I32)
                em.do("vector", lambda g: g.tensor_single_scalar(
                    scrC[:, 0:nsl].bitcast(I32), vi, 0, ALU.is_lt), "ve")
                em.do("vector", lambda g: g.tensor_single_scalar(
                    accS[:, 0:nsl].bitcast(I32), vi, -1, ALU.bitwise_xor), "ve")
                em.do("vector", lambda g: g.tensor_single_scalar(
                    scrA[:, 0:nsl].bitcast(I32), vi, INT_MIN, ALU.bitwise_xor),
                    "ve")
                em.do("vector", lambda g, e2=e_: g.select(
                    mbuf[:, 0:nsl, e2], scrC[:, 0:nsl].bitcast(I32),
                    accS[:, 0:nsl].bitcast(I32), scrA[:, 0:nsl].bitcast(I32)),
                    "ve")

        def part_reduce(em):
            em.do("vector", lambda g: g.tensor_copy(cnt8b[:, :], cnt8[:, :]), "ve")
            em.wait("tensor", "ve")
            em.do("tensor", lambda g: g.matmul(
                pb[6][0:1, 0:E], onescb[:, :], cnt8b[:, :], start=True,
                stop=True), "mm")

        def bcast8(em, row_emit):
            row_emit()
            em.do("vector", lambda g: g.tensor_copy(sel8b[0:1, :], sel8[0:1, :]),
                  "ve")
            em.wait("tensor", "ve")
            em.do("tensor", lambda g: g.matmul(
                pb[7][0:128, 0:E], onesrb[:, :], sel8b[0:1, :], start=True,
                stop=True), "mm")
            em.wait("vector", "mm")

        def radix_search(em, filler=None):
            def fill():
                if filler is not None:
                    filler()

            """Exact per-expert capacity threshold over mall (monotone int32
            M values) using a float hi/lo split: mhi = float(M >> 16) and
            mlo = float(M & 0xffff) are exact in fp32; the radix runs in
            fp32 arithmetic. Produces bhi/blo (threshold), ibs (index tie
            cutoff, floats), and gtp/eqp planes."""
            for e_ in range(E):
                em.do("vector", lambda g, e2=e_: g.tensor_single_scalar(
                    scrA[:, 0:NPL].bitcast(I32), mall[:, 0:NPL, e2], 16,
                    ALU.arith_shift_right), "ve")
                em.do("vector", lambda g, e2=e_: g.tensor_copy(
                    mhi[:, 0:NPL, e2], scrA[:, 0:NPL].bitcast(I32)), "ve")
                em.do("vector", lambda g, e2=e_: g.tensor_single_scalar(
                    scrA[:, 0:NPL].bitcast(I32), mall[:, 0:NPL, e2], 0xFFFF,
                    ALU.bitwise_and), "ve")
                em.do("vector", lambda g, e2=e_: g.tensor_copy(
                    mlo[:, 0:NPL, e2], scrA[:, 0:NPL].bitcast(I32)), "ve")

            def search16(tgt, plane_of, pred_extra, cap_cmp):
                for kbit in range(16):
                    step = float(1 << (15 - kbit))
                    em.do("vector", lambda g, s2=step: g.tensor_scalar(
                        selsh[:, :], tgt[:, :], s2, None, ALU.add), "ve")
                    for e_ in range(E):
                        em.do("vector", lambda g, e2=e_: g.tensor_scalar(
                            junk[:, 0:NPL], plane_of(e2), selsh[:, e2:e2 + 1],
                            0.0, ALU.is_ge, ALU.add,
                            accum_out=cnt8[:, e2:e2 + 1]), "ve")
                        if pred_extra is not None:
                            pred_extra(e_)
                    part_reduce(em)
                    em.wait("vector", "mm")
                    bcast8(em, cap_cmp)
                    em.do("vector", lambda g: g.tensor_copy(
                        selsh[:, :], pb[7][0:128, 0:E]), "ve")
                    em.do("vector", lambda g, s2=step: g.tensor_scalar(
                        selsh[:, :], selsh[:, :], s2, None, ALU.mult), "ve")
                    em.do("vector", lambda g: g.tensor_add(
                        tgt[:, :], tgt[:, :], selsh[:, :]), "ve")
                    fill()

            def cap_ge():
                em.do("vector", lambda g: g.tensor_single_scalar(
                    sel8[0:1, :], pb[6][0:1, 0:E], float(CAP), ALU.is_ge), "ve")

            em.do("vector", lambda g: g.memset(bhi[:, :], -32768.0), "ve")
            search16(bhi, lambda e2: mhi[:, 0:NPL, e2], None, cap_ge)
            for e_ in range(E):
                em.do("vector", lambda g, e2=e_: g.tensor_scalar(
                    junk[:, 0:NPL], mhi[:, 0:NPL, e2], bhi[:, e2:e2 + 1], 0.0,
                    ALU.is_gt, ALU.add, accum_out=mstr[:, e2:e2 + 1]), "ve")
            for e_ in range(E):
                em.do("vector", lambda g, e2=e_: g.tensor_scalar(
                    eqh_pl[:, 0:NPL, e2], mhi[:, 0:NPL, e2], bhi[:, e2:e2 + 1],
                    None, ALU.is_equal), "ve")
            em.do("vector", lambda g: g.memset(blo[:, :], 0.0), "ve")
            for kbit in range(16):
                step = float(1 << (15 - kbit))
                em.do("vector", lambda g, s2=step: g.tensor_scalar(
                    selsh[:, :], blo[:, :], s2, None, ALU.add), "ve")
                for e_ in range(E):
                    em.do("vector", lambda g, e2=e_: g.tensor_scalar(
                        scrA[:, 0:NPL], mlo[:, 0:NPL, e2],
                        selsh[:, e2:e2 + 1], None, ALU.is_ge), "ve")
                    em.do("vector", lambda g, e2=e_: g.tensor_mul(
                        scrA[:, 0:NPL], scrA[:, 0:NPL],
                        eqh_pl[:, 0:NPL, e2]), "ve")
                    em.do("vector", lambda g, e2=e_: g.tensor_reduce(
                        cnt8[:, e2:e2 + 1], scrA[:, 0:NPL], AX.X, ALU.add),
                        "ve")
                em.do("vector", lambda g: g.tensor_add(
                    cnt8[:, :], cnt8[:, :], mstr[:, :]), "ve")
                part_reduce(em)
                em.wait("vector", "mm")
                bcast8(em, cap_ge)
                em.do("vector", lambda g: g.tensor_copy(
                    selsh[:, :], pb[7][0:128, 0:E]), "ve")
                em.do("vector", lambda g, s2=step: g.tensor_scalar(
                    selsh[:, :], selsh[:, :], s2, None, ALU.mult), "ve")
                em.do("vector", lambda g: g.tensor_add(
                    blo[:, :], blo[:, :], selsh[:, :]), "ve")
                fill()

            for e_ in range(E):
                em.do("vector", lambda g, e2=e_: g.tensor_scalar(
                    scrA[:, 0:NPL], mlo[:, 0:NPL, e2], blo[:, e2:e2 + 1],
                    None, ALU.is_gt), "ve")
                em.do("vector", lambda g, e2=e_: g.tensor_mul(
                    scrA[:, 0:NPL], scrA[:, 0:NPL], eqh_pl[:, 0:NPL, e2]),
                    "ve")
                em.do("vector", lambda g, e2=e_: g.tensor_scalar(
                    scrC[:, 0:NPL], mhi[:, 0:NPL, e2], bhi[:, e2:e2 + 1],
                    None, ALU.is_gt), "ve")
                em.do("vector", lambda g, e2=e_: g.tensor_add(
                    gtp[:, 0:NPL, e2], scrA[:, 0:NPL], scrC[:, 0:NPL]), "ve")
                em.do("vector", lambda g, e2=e_: g.tensor_scalar(
                    scrC[:, 0:NPL], mlo[:, 0:NPL, e2], blo[:, e2:e2 + 1],
                    None, ALU.is_equal), "ve")
                em.do("vector", lambda g, e2=e_: g.tensor_mul(
                    eqp[:, 0:NPL, e2], scrC[:, 0:NPL], eqh_pl[:, 0:NPL, e2]),
                    "ve")
                em.do("vector", lambda g, e2=e_: g.tensor_scalar(
                    junk[:, 0:NPL], gtp[:, 0:NPL, e2], 1.0, 0.0, ALU.mult,
                    ALU.add, accum_out=cnt8[:, e2:e2 + 1]), "ve")
            part_reduce(em)
            em.wait("vector", "mm")
            bcast8(em, lambda: em.do("vector", lambda g: g.tensor_scalar(
                sel8[0:1, :], pb[6][0:1, 0:E], -1.0, float(CAP),
                ALU.mult, ALU.add), "ve"))
            em.do("vector", lambda g: g.tensor_copy(
                need8[:, :], pb[7][0:128, 0:E]), "ve")
            em.do("vector", lambda g: g.memset(ibs[:, :], 0.0), "ve")
            for kbit in range(16):
                step = float(1 << (15 - kbit))
                em.do("vector", lambda g, s2=step: g.tensor_scalar(
                    selsh[:, :], ibs[:, :], s2, None, ALU.add), "ve")
                for e_ in range(E):
                    em.do("vector", lambda g, e2=e_: g.tensor_scalar(
                        scrA[:, 0:NPL], idxp[:, 0:NPL], selsh[:, e2:e2 + 1],
                        None, ALU.is_lt), "ve")
                    em.do("vector", lambda g, e2=e_: g.tensor_mul(
                        scrA[:, 0:NPL], scrA[:, 0:NPL], eqp[:, 0:NPL, e2]),
                        "ve")
                    em.do("vector", lambda g, e2=e_: g.tensor_reduce(
                        cnt8[:, e2:e2 + 1], scrA[:, 0:NPL], AX.X, ALU.add),
                        "ve")
                part_reduce(em)
                em.wait("vector", "mm")
                bcast8(em, lambda: em.do("vector", lambda g: g.tensor_tensor(
                    sel8[0:1, :], pb[6][0:1, 0:E], need8[0:1, :], ALU.is_le),
                    "ve"))
                em.do("vector", lambda g: g.tensor_copy(
                    selsh[:, :], pb[7][0:128, 0:E]), "ve")
                em.do("vector", lambda g, s2=step: g.tensor_scalar(
                    selsh[:, :], selsh[:, :], s2, None, ALU.mult), "ve")
                em.do("vector", lambda g: g.tensor_add(
                    ibs[:, :], ibs[:, :], selsh[:, :]), "ve")
                fill()

        def keep_all(em, with_probs):
            neg_hi = float(M_NEG >> 16)
            neg_lo = float(M_NEG & 0xFFFF)
            for e_ in range(E):
                em.do("vector", lambda g, e2=e_: g.tensor_scalar(
                    scrA[:, 0:NPL], idxp[:, 0:NPL], ibs[:, e2:e2 + 1],
                    None, ALU.is_lt), "ve")
                em.do("vector", lambda g, e2=e_: g.tensor_mul(
                    scrA[:, 0:NPL], scrA[:, 0:NPL], eqp[:, 0:NPL, e2]), "ve")
                em.do("vector", lambda g, e2=e_: g.tensor_add(
                    scrA[:, 0:NPL], scrA[:, 0:NPL], gtp[:, 0:NPL, e2]), "ve")
                em.do("vector", lambda g, e2=e_: g.tensor_single_scalar(
                    scrB[:, 0:NPL], mhi[:, 0:NPL, e2], neg_hi, ALU.is_gt), "ve")
                em.do("vector", lambda g, e2=e_: g.tensor_single_scalar(
                    scrC[:, 0:NPL], mhi[:, 0:NPL, e2], neg_hi, ALU.is_equal),
                    "ve")
                em.do("vector", lambda g, e2=e_: g.tensor_single_scalar(
                    junk[:, 0:NPL], mlo[:, 0:NPL, e2], neg_lo, ALU.is_gt), "ve")
                em.do("vector", lambda g: g.tensor_mul(
                    scrC[:, 0:NPL], scrC[:, 0:NPL], junk[:, 0:NPL]), "ve")
                em.do("vector", lambda g: g.tensor_add(
                    scrB[:, 0:NPL], scrB[:, 0:NPL], scrC[:, 0:NPL]), "ve")
                em.do("vector", lambda g: g.tensor_mul(
                    scrA[:, 0:NPL], scrA[:, 0:NPL], scrB[:, 0:NPL]), "ve")
                if with_probs:
                    em.do("vector", lambda g, e2=e_: g.tensor_mul(
                        sall[:, 0:NPL, e2], scrA[:, 0:NPL], pall[:, 0:NPL, e2]),
                        "ve")
                else:
                    em.do("vector", lambda g, e2=e_: g.tensor_copy(
                        sall[:, 0:NPL, e2], scrA[:, 0:NPL]), "ve")

        def extract_own(em, mul_p3):
            ve_cp = em.val("ve")
            em.wait("sync", "ve", ve_cp)
            st_cp = em.do("sync", lambda g: g.dma_start(out=sald.ap(), in_=sall),
                          "st", 16)
            em.wait("sync", "st", st_cp)

            def dyn(g):
                cid = nc.partition_id([mybir.EngineType.SP])
                return g.dma_start(
                    out=ssb[:, :, :],
                    in_=sald.ap().rearrange("p a b -> p (a b)")
                        [:, bass.ts(cid, NTI * E)])
            ld_cp = em.do("sync", dyn, "ld", 16)
            em.wait("vector", "ld", ld_cp)
            if mul_p3:
                for e_ in range(E):
                    em.do("vector", lambda g, e2=e_: g.tensor_mul(
                        ssb[:, 0:NTI, e2], ssb[:, 0:NTI, e2],
                        pall[:, 0:NTI, e2]), "ve")

        def launch_a_p0(em):
            em.do("gpsimd", lambda g: g.dma_start(out=relb[:, :], in_=rel.ap()),
                  "lg", 16)
            em.do("gpsimd", lambda g: g.dma_start(
                out=gwob[:, :, :],
                in_=gwo.ap().rearrange("(dt p) g -> p dt g", p=128)), "lg", 16)
            lg_rel = em.val("lg")
            em.do("sync", lambda g: g.dma_start(out=antib[:, :], in_=antid_t.ap()),
                  "ld", 16)
            em.do("sync", lambda g: g.dma_start(out=ohb[:, :], in_=oh_t.ap()),
                  "ld", 16)
            ld_cst = em.val("ld")
            em.wait("tensor", "lg", lg_rel)
            em.wait("tensor", "ld", ld_cst)
            em.do("tensor", lambda g: g.matmul(
                pb[0][0:16, 0:GW], relb[:, :], ohb[:, :], start=True, stop=True),
                "mm")
            em.wait("vector", "mm")
            em.do("vector", lambda g: g.tensor_copy(grow, pb[0][0:16, 0:GW]), "ve")
            ve_g = em.val("ve")
            em.wait("sync", "ve", ve_g)
            st_g = em.do("sync", lambda g: g.dma_start(out=gtd.ap(), in_=grow),
                         "st", 16)
            for h_ in range(H):
                em.wait("sync", "st", st_g)
                em.wait("sync", "mm")
                ld_h = em.do("sync", lambda g, h2=h_: g.dma_start(
                    out=hh, in_=AP(tensor=gtd, offset=h2 * GW + GOFF,
                                   ap=[[1, 128], [1, WWIN]])), "ld", 16)
                em.wait("tensor", "ld", ld_h)
                em.wait("tensor", "se")
                em.do("tensor", lambda g: g.matmul(
                    pb[1][0:128, 0:WWIN], antib[:, :], hh, start=True, stop=True),
                    "mm")
                mm_cp = em.val("mm")
                em.wait("scalar", "mm", mm_cp)
                em.do("scalar", lambda g, h2=h_: g.activation(
                    bband[:, h2, 0:WWIN], pb[1][0:128, 0:WWIN], AF.Exp), "se")
            em.wait("gpsimd", "lg", lg_rel)
            em.do("gpsimd", lambda g: g.dma_start(
                out=cv32[0:1, 0:16], in_=relb[31:32, :]), "lg", 16)
            cv_cp = em.do("gpsimd", lambda g: g.dma_start(
                out=cv32[0:1, 16:32], in_=relb[15:16, :]), "lg", 16)
            em.wait("tensor", "lg", cv_cp)
            em.do("tensor", lambda g: g.matmul(
                pb[2][0:128, 0:32], onesrb[:, :], cv32[0:1, :], start=True,
                stop=True), "mm")
            em.wait("vector", "mm")
            em.do("vector", lambda g: g.tensor_scalar_add(
                cbias[:, :], pb[2][0:128, 0:32], -CSH), "ve")
            for i in range(32):
                em.do("vector", lambda g, i2=i: g.tensor_copy(
                    cbcols[i2][:, :], cbias[:, i2:i2 + 1]), "ve")

        def launch_a_p1(em):
            em.wait("tensor", "lg")
            for dt in range(8):
                em.do("tensor", lambda g, d2=dt: g.matmul(
                    pb[4][0:1, 0:16], onescb[:, :], gwob[:, d2, :],
                    start=(d2 == 0), stop=(d2 == 7)), "mm")
            em.wait("vector", "mm")
            em.do("vector", lambda g: g.tensor_copy(g19[0:1, :], pb[4][0:1, 0:16]),
                  "ve")
            em.do("vector", lambda g: g.tensor_copy(g19b[0:1, :], g19[0:1, :]),
                  "ve")
            ve_cp = em.val("ve")
            em.wait("tensor", "ve", ve_cp)
            em.do("tensor", lambda g: g.matmul(
                pb[4][0:128, 0:16], onesrb[:, :], g19b[0:1, :], start=True,
                stop=True), "mm")
            em.wait("vector", "mm")
            em.do("vector", lambda g: g.tensor_copy(
                gb8[:, 0:E], pb[4][0:128, 0:E]), "ve")

            ld_cps = {}
            ve_cast_done = {}
            mm_done = {-1: 0, -2: 0}
            sq_mm = {}
            evac_ve = {}

            def p1_load(gc):
                bi = gc % 2
                ld_cps[gc] = em.do("sync", lambda g, c2=gc, b3=bi: g.dma_start(
                    out=p1s2[b3],
                    in_=x0T.ap().rearrange("(dt p) n -> p dt n", p=128)
                        [:, :, c2 * QCH:(c2 + 1) * QCH]), "ld", 16)

            def p1_transposes(gc):
                for sub in range(QCH // 128):
                    em.wait("tensor", "ve",
                            evac_ve[gc] if sub == 0 else em.val("ve"))
                    em.do("tensor", lambda g, s2=sub, b3=gc % 2: g.transpose(
                        pb[3][0:128, 0:17].bitcast(BF16)[:, 0:33],
                        st10_2[b3][0:33, s2 * 128:(s2 + 1) * 128],
                        idb[0:33, 0:33]), "mm")
                    em.wait("vector", "mm")
                    em.do("vector", lambda g, s2=sub, c2=gc: g.tensor_copy(
                        rawT[:, 0:9, c2 * (QCH // 128) + s2],
                        pb[3][0:128, 0:17].bitcast(BF16)[:, 0:9]), "ve")
                    em.do("vector", lambda g, s2=sub, c2=gc: g.tensor_copy(
                        rawT[:, 9:10, c2 * (QCH // 128) + s2],
                        pb[3][0:128, 0:17].bitcast(BF16)[:, 32:33]), "ve",
                        chain=False)

            p1_load(0)
            for gc in range(GCH):
                bi = gc % 2
                ps_ = p1s2[bi]
                pbx = p1b2[bi]
                if gc + 1 < GCH:
                    em.wait("sync", "ve", ve_cast_done.get(gc - 1, 0))
                    p1_load(gc + 1)
                em.wait("vector", "ld", ld_cps[gc])
                em.wait("vector", "mm", mm_done[gc - 2])
                for dt in range(8):
                    em.do("vector", lambda g, d2=dt, p4=pbx, p5=ps_:
                          g.tensor_copy(p4[:, d2, :], p5[:, d2, :]), "ve",
                          chain=False)
                ve_cast = em.val("ve")
                ve_cast_done[gc] = ve_cast
                em.wait("tensor", "ve", ve_cast)
                for dt in range(8):
                    em.do("tensor", lambda g, d2=dt, p4=pbx: g.matmul(
                        pb[0][0:16, 0:QCH], gwob[:, d2, :], p4[:, d2, :],
                        start=(d2 == 0), stop=(d2 == 7)), "mm")
                    em.wait("vector", "mm", sq_mm.get((gc * 8 + dt) - 2, 0))
                    em.do("vector", lambda g, d2=dt, p4=pbx: g.tensor_mul(
                        p1sqb2[:, d2 % 2, :], p4[:, d2, :], p4[:, d2, :]),
                        "ve", chain=False)
                    ve_cp2 = em.val("ve")
                    em.wait("tensor", "ve", ve_cp2)
                    em.do("tensor", lambda g, d2=dt: g.matmul(
                        pb[1][0:1, 0:QCH], onescb[:, :], p1sqb2[:, d2 % 2, :],
                        start=(d2 == 0), stop=(d2 == 7)), "mm")
                    sq_mm[gc * 8 + dt] = em.val("mm")
                mm_cp = em.val("mm")
                em.wait("vector", "mm", mm_cp)
                em.do("vector", lambda g, b3=bi: g.tensor_copy(
                    st10_2[b3][0:9, :], pb[0][0:9, 0:QCH]), "ve")
                em.do("vector", lambda g, b3=bi: g.tensor_copy(
                    st10_2[b3][32:33, :], pb[1][0:1, 0:QCH]), "ve",
                    chain=False)
                evac_ve[gc] = em.val("ve")
                mm_done[gc] = em.val("mm")
                p1_transposes(gc)

            em.do("vector", lambda g: g.tensor_scalar_mul(
                accS[:, 0:NPL], rawT[:, 8, :], 1.0 / D), "ve")
            em.do("vector", lambda g: g.tensor_scalar_mul(
                accR[:, 0:NPL], rawT[:, 9, :], 1.0 / D), "ve")
            em.do("vector", lambda g: g.tensor_mul(
                scrB[:, 0:NPL], accS[:, 0:NPL], accS[:, 0:NPL]), "ve")
            em.do("vector", lambda g: g.tensor_sub(
                accR[:, 0:NPL], accR[:, 0:NPL], scrB[:, 0:NPL]), "ve")
            em.do("vector", lambda g: g.tensor_scalar_add(
                accR[:, 0:NPL], accR[:, 0:NPL], float(EPS)), "ve")
            ve_cp = em.val("ve")
            em.wait("scalar", "ve", ve_cp)
            em.do("scalar", lambda g: g.activation(
                scrB[:, 0:NPL], accR[:, 0:NPL], AF.Ln), "se")
            em.do("scalar", lambda g: g.activation(
                scrB[:, 0:NPL], scrB[:, 0:NPL], AF.Exp, scale=-0.5), "se")
            se_cp = em.val("se")
            em.wait("vector", "se", se_cp)
            for e_ in range(E):
                em.do("vector", lambda g, e2=e_: g.tensor_single_scalar(
                    junk[:, 0:NPL], accS[:, 0:NPL], gb8[:, e2:e2 + 1],
                    ALU.mult), "ve")
                em.do("vector", lambda g, e2=e_: g.tensor_sub(
                    junk[:, 0:NPL], rawT[:, e2, :], junk[:, 0:NPL]), "ve")
                em.do("vector", lambda g, e2=e_: g.tensor_mul(
                    pall[:, 0:NPL, e2], junk[:, 0:NPL], scrB[:, 0:NPL]), "ve")

        def attention(em, skip_ln=False):
            if not skip_ln:
                prev = [0, 0]
                for t in range(NTI):
                    em.wait("sync", "ve", prev[0])
                    em.wait("sync", "se", prev[1])
                    ve_c, se_b, _ = ln_tile(em, x1d, t)
                    for dt in range(8):
                        em.wait("tensor", "ve",
                                ve_c if dt == 0 else em.val("ve"))
                        em.do("tensor", lambda g, d2=dt: g.transpose(
                            tview(3), xnb[:, d2 * 128:(d2 + 1) * 128],
                            idb[:, :]), "mm")
                        em.wait("vector", "mm")
                        em.do("vector", lambda g, d2=dt, tt=t: g.tensor_copy(
                            xnT2[:, d2, tt * 128:(tt + 1) * 128], tview(3)),
                            "ve", chain=False)
                    prev = [em.val("ve"), se_b]
            xn2_ready_ve = em.val("ve")

            for vg in range(H // 4):
                em.wait("gpsimd", "mm")
                lg_v = em.do("gpsimd", lambda g, v2=vg: g.dma_start(
                    out=wvb, in_=wv.ap().rearrange("(dt p) d -> p dt d", p=128)
                        [:, :, v2 * 256:(v2 + 1) * 256]), "lg", 16)
                em.wait("tensor", "lg", lg_v)
                em.wait("tensor", "ve", xn2_ready_ve)
                for t in range(KTI):
                    em.wait("tensor", "ve")
                    for dt in range(8):
                        em.do("tensor", lambda g, t2=t, d2=dt: g.matmul(
                            pb[6][0:128, 0:256],
                            xnT2[:, d2, t2 * 128:(t2 + 1) * 128],
                            wvb[:, d2, :], start=(d2 == 0), stop=(d2 == 7)),
                            "mm")
                    mm_cp = em.val("mm")
                    em.wait("vector", "mm", mm_cp)
                    em.do("vector", lambda g, t2=t: g.tensor_copy(
                        vsb[:, t2, :, 0:64],
                        pb[6][0:128, 0:256].rearrange("p (a b) -> p a b", b=64)),
                        "ve", chain=False)
                em.do("vector", lambda g: g.memset(vsb[:, :, :, 64:65], 1.0), "ve")
                v_ready_ve = em.val("ve")
                v_evac = v_ready_ve
                for pp in range(2):
                    attn_pair(em, vg * 2 + pp, pp, v_evac, v_ready_ve,
                              xn2_ready_ve)

            if DEBUG:
                em.wait("sync", "ve")
                em.wait("sync", "se")
                em.do("sync", lambda g: g.dma_start(
                    out=dbg_den.ap(), in_=den[0:1, :]), "st", 16)
                em.wait("sync", "st")
            em.wait("gpsimd", "mm")
            lg_wo = em.do("gpsimd", lambda g: g.dma_start(
                out=wob, in_=wo.ap().rearrange("(ct p) d -> p ct d", p=128)),
                "lg", 16)
            prev_ld = 0
            prev_reads = [0, 0, 0]
            prev_x2n = [0]
            for ch in range(NCH):
                em.wait("sync", "mm", prev_ld)
                ld_ctx = em.do("sync", lambda g, c2=ch: g.dma_start(
                    out=ctxc, in_=ctxd.ap()[:, :, c2 * QCH:(c2 + 1) * QCH]
                        .rearrange("a p t -> p a t")), "ld", 16)
                for st_ in range(QCH // 128):
                    t = ch * (QCH // 128) + st_
                    em.wait("sync", "ve", prev_reads[0])
                    em.wait("sync", "se", prev_reads[1])
                    em.wait("sync", "st", prev_reads[2])
                    ld_x = em.do("sync", lambda g, tt=t: g.dma_start(
                        out=xtile2[:, :],
                        in_=x1d.ap()[tt * 128:(tt + 1) * 128, :]), "ld", 16)
                    em.wait("tensor", "ld", ld_ctx)
                    em.wait("tensor", "lg", lg_wo)
                    for dh in range(2):
                        bank = dh % 2
                        em.wait("tensor", "ve", prev_reads[0])
                        for ct in range(8):
                            em.do("tensor", lambda g, c3=ct, s2=st_, d2=dh,
                                  b2=bank: g.matmul(
                                      pb[b2][0:128, 0:512],
                                      ctxc[:, c3, s2 * 128:(s2 + 1) * 128],
                                      wob[:, c3, d2 * 512:(d2 + 1) * 512],
                                      start=(c3 == 0), stop=(c3 == 7)), "mm")
                        mm_cp = em.val("mm")
                        em.wait("vector", "mm", mm_cp)
                        em.wait("vector", "ld", ld_x)
                        em.do("vector", lambda g, d2=dh, b2=bank:
                              g.scalar_tensor_tensor(
                                  xtile[:, d2 * 512:(d2 + 1) * 512],
                                  pb[b2][0:128, 0:512], 1.0,
                                  xtile2[:, d2 * 512:(d2 + 1) * 512],
                                  ALU.mult, ALU.add), "ve", chain=False)
                    ve_x2 = em.val("ve")
                    em.wait("sync", "ve", ve_x2)
                    st_cp = em.do("sync", lambda g, tt=t: g.dma_start(
                        out=x_out.ap()[tt * 128:(tt + 1) * 128, :],
                        in_=xtile[:, :]), "st", 16)
                    em.do("vector", lambda g: g.reduce_sum(
                        stt[:, 0:1], xtile[:, :], AX.X), "ve")
                    em.wait("scalar", "ve")
                    em.do("scalar", lambda g: g.activation(
                        sq[:, :], xtile[:, :], AF.Square, accum_out=stt[:, 1:2]),
                        "se")
                    se_a = em.val("se")
                    em.wait("vector", "se", se_a)
                    em.do("vector", lambda g: g.tensor_scalar_mul(
                        stt[:, 2:3], stt[:, 0:1], 1.0 / D), "ve")
                    em.do("vector", lambda g: g.tensor_scalar_mul(
                        stt[:, 3:4], stt[:, 1:2], 1.0 / D), "ve")
                    em.do("vector", lambda g: g.tensor_mul(
                        stt[:, 4:5], stt[:, 2:3], stt[:, 2:3]), "ve")
                    em.do("vector", lambda g: g.tensor_sub(
                        stt[:, 3:4], stt[:, 3:4], stt[:, 4:5]), "ve")
                    em.do("vector", lambda g: g.tensor_scalar_add(
                        stt[:, 3:4], stt[:, 3:4], float(EPS)), "ve")
                    ve_b = em.val("ve")
                    em.wait("scalar", "ve", ve_b)
                    em.do("scalar", lambda g: g.activation(
                        stt[:, 5:6], stt[:, 3:4], AF.Ln), "se")
                    em.do("scalar", lambda g: g.activation(
                        stt[:, 6:7], stt[:, 5:6], AF.Exp, scale=-0.5), "se")
                    se_b = em.val("se")
                    em.wait("vector", "se", se_b)
                    em.wait("vector", "st", prev_x2n[0])
                    em.do("vector", lambda g: g.tensor_scalar(
                        xnb[:, :], xtile[:, :], stt[:, 2:3], stt[:, 6:7],
                        ALU.subtract, ALU.mult), "ve")
                    ve_c = em.val("ve")
                    em.wait("sync", "ve", ve_c)
                    prev_x2n[0] = em.do("sync", lambda g, tt=t: g.dma_start(
                        out=x2n.ap()[tt * 128:(tt + 1) * 128, :],
                        in_=xnb[:, :]), "st", 16)
                    ve_d = transpose_xnb(em, ve_c)
                    em.wait("tensor", "ve", ve_d)
                    for dt in range(8):
                        em.do("tensor", lambda g, d2=dt: g.matmul(
                            pb[5][0:128, 0:16], xnstage[:, d2, :],
                            gwob[:, d2, :], start=(d2 == 0), stop=(d2 == 7)),
                            "mm")
                    mm_cp = em.val("mm")
                    em.wait("scalar", "mm", mm_cp)
                    em.do("scalar", lambda g, tt=t: g.copy(
                        l3[:, tt, 0:E], pb[5][0:128, 0:E]), "se", chain=False)
                    prev_reads = [ve_d, em.val("se"), st_cp]
                prev_ld = em.val("mm")

        def attn_pair(em, pair, pp, v_evac, v_ready_ve, xn2_ready_ve):
            em.wait("gpsimd", "mm")
            em.do("gpsimd", lambda g, p2=pair: g.dma_start(
                out=wqb, in_=wq.ap().rearrange("(dt p) d -> p dt d", p=128)
                    [:, :, p2 * 128:(p2 + 1) * 128]), "lg", 16)
            lg_qk = em.do("gpsimd", lambda g, p2=pair: g.dma_start(
                out=wkb, in_=wk.ap().rearrange("(dt p) d -> p dt d", p=128)
                    [:, :, p2 * 128:(p2 + 1) * 128]), "lg", 16)
            em.wait("tensor", "lg", lg_qk)
            em.wait("tensor", "ve", xn2_ready_ve)
            for ch in range(NCH):
                for wsl, dst in ((wqb, qt2), (wkb, kt2s)):
                    em.wait("tensor", "ve")
                    for dt in range(8):
                        em.do("tensor", lambda g, d2=dt, c2=ch, w2=wsl: g.matmul(
                            pb[6][0:128, 0:QCH], w2[:, d2, :],
                            xnT2[:, d2, c2 * QCH:(c2 + 1) * QCH],
                            start=(d2 == 0), stop=(d2 == 7)), "mm")
                    mm_cp = em.val("mm")
                    em.wait("vector", "mm", mm_cp)
                    em.do("vector", lambda g, c2=ch, d3=dst: g.tensor_copy(
                        d3[:, c2 * QCH:(c2 + 1) * QCH], pb[6][0:128, 0:QCH]),
                        "ve", chain=False)
            qk_ready = em.val("ve")

            NG = NCH * KTI
            qk_mm = {}
            exp_cp = {}
            av_cp = {}
            band_cp = {}
            ctr_done = {-1: 0}
            norm_ready = {}

            def emit_qk(gkt, first=False):
                ch2, kt = divmod(gkt, KTI)
                q0 = ch2 * QCH
                k0 = kt * 128
                for hh_ in range(2):
                    sbank = hh_ * 2 + (gkt % 2)
                    tp = (0, 0) if hh_ == 0 else (64, 0)
                    r0 = hh_ * 64
                    if first:
                        em.wait("tensor", "ve", qk_ready)
                    em.wait("tensor", "se", exp_cp.get((hh_, gkt - 2), 0))
                    em.do("tensor", lambda g, b2=sbank, rr=r0, k2=k0, c2=q0,
                          t2=tp: g.matmul(
                              pb[b2][0:128, 0:QCH],
                              kt2s[rr:rr + 64, k2:k2 + 128],
                              qt2[rr:rr + 64, c2:c2 + QCH],
                              start=True, stop=True, tile_position=t2), "mm")
                    qk_mm[(hh_, gkt)] = em.val("mm")

            def emit_norm(ch2):
                for hh_ in range(2):
                    em.wait("tensor", "ve",
                            norm_ready[ch2] if hh_ == 0 else em.val("ve"))
                    em.do("tensor", lambda g, h3=hh_: g.matmul(
                        pb[7][0:64, 0:QCH], onesrb[0:1, 0:64],
                        denb_h[h3][0:1, :], start=True, stop=True), "mm")
                    mm_cp2 = em.val("mm")
                    em.wait("vector", "mm", mm_cp2)
                    em.do("vector", lambda g, rr=hh_ * 64, c2=ch2 * QCH, h3=hh_:
                          g.tensor_tensor(
                              ctxp[rr:rr + 64, c2:c2 + QCH], ctr_h[h3][:, 0:QCH],
                              pb[7][0:64, 0:QCH], ALU.mult), "ve")

            emit_qk(0, first=True)
            for gkt in range(NG):
                ch2, kt = divmod(gkt, KTI)
                k0 = kt * 128
                q0 = ch2 * QCH
                a = k0 - 90 - q0
                b = a + WWIN
                a_c = min(max(a, 0), QCH)
                b_c = min(max(b, 0), QCH)
                for hh_ in range(2):
                    h_ = pair * 2 + hh_
                    sbank = hh_ * 2 + (gkt % 2)
                    wslot = wtb[:, hh_, gkt % 2, :]
                    em.wait("scalar", "mm", max(qk_mm[(hh_, gkt)],
                                                av_cp.get((hh_, gkt - 2), 0)))
                    if a_c > 0:
                        em.do("scalar", lambda g, b2=sbank, h2=h_, ac=a_c,
                              w3=wslot: g.activation(
                                  w3[:, 0:ac], pb[b2][0:128, 0:ac], AF.Exp,
                                  bias=cbcols[h2][:, :]), "se", chain=False)
                    if a_c < b_c:
                        em.do("scalar", lambda g, b2=sbank, ac=a_c, bc=b_c,
                              w3=wslot: g.activation(
                                  w3[:, ac:bc], pb[b2][0:128, ac:bc],
                                  AF.Exp, bias=cshcol[:, 0:1]), "se",
                              chain=False)
                    if b_c < QCH:
                        em.do("scalar", lambda g, b2=sbank, h2=h_, bc=b_c,
                              w3=wslot: g.activation(
                                  w3[:, bc:QCH], pb[b2][0:128, bc:QCH],
                                  AF.Exp, bias=cbcols[16 + h2][:, :]),
                              "se", chain=False)
                    exp_cp[(hh_, gkt)] = em.val("se")
                    if a_c < b_c:
                        off = a_c - a
                        em.wait("vector", "se", exp_cp[(hh_, gkt)])
                        em.do("vector", lambda g, h2=h_, o2=off, ac=a_c,
                              bc=b_c, w3=wslot: g.tensor_mul(
                                  w3[:, ac:bc], w3[:, ac:bc],
                                  bband[:, h2, o2:o2 + (bc - ac)]), "ve",
                              chain=False)
                        band_cp[(hh_, gkt)] = em.val("ve")
                if kt == 1 and ch2 > 0:
                    emit_norm(ch2 - 1)
                if gkt + 1 < NG:
                    emit_qk(gkt + 1)
                for hh_ in range(2):
                    wslot = wtb[:, hh_, gkt % 2, :]
                    em.wait("tensor", "se", exp_cp[(hh_, gkt)])
                    em.wait("tensor", "ve", band_cp.get((hh_, gkt), 0))
                    if kt == 0:
                        em.wait("tensor", "ve", v_ready_ve)
                        em.wait("tensor", "ve", ctr_done[ch2 - 1])
                    em.do("tensor", lambda g, b2=4 + hh_, t2=kt,
                          h4=pp * 2 + hh_, w3=wslot, first=(kt == 0),
                          last=(kt == KTI - 1): g.matmul(
                              pb[b2][0:65, 0:QCH], vsb[:, t2, h4, 0:65],
                              w3[:, :], start=first, stop=last), "mm")
                    av_cp[(hh_, gkt)] = em.val("mm")
                if kt == KTI - 1:
                    for hh_ in range(2):
                        em.wait("vector", "mm", av_cp[(hh_, gkt)])
                        em.do("vector", lambda g, h3=hh_: g.tensor_copy(
                            den_h[h3][0:1, :], pb[4 + h3][64:65, 0:QCH]), "ve")
                        em.do("vector", lambda g, h3=hh_: g.tensor_copy(
                            ctr_h[h3][:, 0:QCH], pb[4 + h3][0:64, 0:QCH]),
                            "ve", chain=False)
                    ctr_done[ch2] = em.val("ve")
                    for hh_ in range(2):
                        em.do("vector", lambda g, h3=hh_: g.reciprocal(
                            den_h[h3][0:1, :], den_h[h3][0:1, :]), "ve")
                        em.do("vector", lambda g, h3=hh_: g.tensor_copy(
                            denb_h[h3][0:1, :], den_h[h3][0:1, :]), "ve")
                    norm_ready[ch2] = em.val("ve")
            emit_norm(NCH - 1)
            ve_cp = em.val("ve")
            em.wait("sync", "ve", ve_cp)
            st_cp = em.do("sync", lambda g, p2=pair: g.dma_start(
                out=ctxd.ap()[p2], in_=ctxp), "st", 16)
            em.wait("vector", "st", st_cp)

        def program(em: Emit):
            em.do("sync", lambda g: g.dma_start(out=idb[:, :], in_=ident_t.ap()),
                  "ld", 16)
            em.do("sync", lambda g: g.dma_start(out=onesrb[:, :],
                                                in_=ones1_t.ap()), "ld", 16)
            em.do("sync", lambda g: g.dma_start(out=onescb[:, :],
                                                in_=onesc_t.ap()), "ld", 16)
            em.do("sync", lambda g: g.dma_start(out=onescf[:, :],
                                                in_=onescf_t.ap()), "ld", 16)
            em.do("sync", lambda g: g.dma_start(out=idxp[:, :], in_=idx_t.ap()),
                  "ld", 16)
            if launch == "B":
                em.do("gpsimd", lambda g: g.dma_start(
                    out=gwob[:, :, :],
                    in_=gwo.ap().rearrange("(dt p) g -> p dt g", p=128)),
                    "lg", 16)
            em.do("vector", lambda g: g.memset(negp[:, :], NEG), "ve")
            em.do("vector", lambda g: g.memset(cshcol[:, :], -CSH), "ve")
            zf_st = 0
            if SPARSE:
                em.do("sync", lambda g: g.dma_start(out=ultrib[:, :],
                                                    in_=ultri_t.ap()), "ld", 16)
                em.do("sync", lambda g: g.dma_start(out=onesrf[:, :],
                                                    in_=onesrf_t.ap()), "ld", 16)
                em.do("vector", lambda g: g.memset(zbp[:, :], 0.0), "ve")
                zf_ve = em.val("ve")
                em.wait("sync", "ve", zf_ve)
                for e_ in range(E):
                    for r in range(SLOTS // 128):
                        em.do("sync", lambda g, e2=e_, r2=r: g.dma_start(
                            out=xgd.ap()[e2][r2 * 128:(r2 + 1) * 128, :],
                            in_=zbp[:, :]), "st", 16)
                zf_st = em.val("st")
            if launch == "A":
                launch_a_p0(em)
                em.barrier()
                launch_a_p1(em)
                em.barrier()
                if DEBUG:
                    em.do("sync", lambda g: g.dma_start(
                        out=dbg_raw.ap(), in_=rawT[:, 0:10, :]), "st", 16)
                    em.do("sync", lambda g: g.dma_start(
                        out=dbg_gb8.ap(), in_=gb8[:, :]), "st", 16)
                    em.do("sync", lambda g: g.dma_start(
                        out=dbg_log.ap(), in_=pall[:, :, :]), "st", 16)
                    em.do("sync", lambda g: g.dma_start(
                        out=dbg_m.ap(), in_=accS[:, 0:NPL]), "st", 16)
                    em.do("sync", lambda g: g.dma_start(
                        out=dbg_rs.ap(), in_=scrB[:, 0:NPL]), "st", 16)
                    em.wait("sync", "st")
                    em.barrier()
                probs_inplace(em, pall, NPL)
                routing_math(em, pall, mall, NPL)
                em.barrier()
                ln_step = ln_spill_stepper(em, x_in)
                radix_search(em, filler=ln_step)
                keep_all(em, with_probs=True)
                extract_own(em, mul_p3=False)
                while ln_step():
                    pass
                em.barrier()
                if DEBUG:
                    em.do("sync", lambda g: g.dma_start(
                        out=dbg_mall.ap(), in_=mall[:, :, :]), "st", 16)
                    em.do("sync", lambda g: g.dma_start(
                        out=dbg_thr.ap()[:, 0, :], in_=bhi[:, :]), "st", 16)
                    em.do("sync", lambda g: g.dma_start(
                        out=dbg_thr.ap()[:, 1, :], in_=blo[:, :]), "st", 16)
                    em.do("sync", lambda g: g.dma_start(
                        out=dbg_thr.ap()[:, 2, :], in_=ibs[:, :].bitcast(F32)
                        if False else ibs[:, :]), "st", 16)
                    em.do("sync", lambda g: g.dma_start(
                        out=dbg_thr.ap()[:, 3, :], in_=need8[:, :]), "st", 16)
                    em.do("sync", lambda g: g.dma_start(
                        out=dbg_sall.ap(), in_=sall[:, :, :]), "st", 16)
                    em.wait("sync", "st")
                    em.barrier()
                if SPARSE:
                    ffn_sparse(em, x_in, x1d, zf_st, ln_sink=True)
                else:
                    ffn(em, x_in, x1d)
                em.barrier()
                attention(em, skip_ln=SPARSE)
                em.barrier()
                probs_inplace(em, l3, NTI)
                routing_math(em, l3, mall, NTI)
                if DEBUG:
                    em.do("sync", lambda g: g.dma_start(
                        out=dbg_ctx.ap(), in_=ctxd.ap()), "st", 16)
                    em.do("sync", lambda g: g.dma_start(
                        out=dbg_qt.ap(), in_=qt2), "st", 16)
                    em.do("sync", lambda g: g.dma_start(
                        out=dbg_kt.ap(), in_=kt2s), "st", 16)
                    em.do("sync", lambda g: g.dma_start(
                        out=dbg_v.ap(), in_=vsb[:, :, :, :]), "st", 16)
                    em.do("sync", lambda g: g.dma_start(
                        out=dbg_xnt2.ap(), in_=xnT2[:, :, :]), "st", 16)
                    em.do("sync", lambda g: g.dma_start(
                        out=dbg_wtb.ap(), in_=wtb[:, :, :, :]), "st", 16)
                    em.do("sync", lambda g: g.dma_start(
                        out=dbg_den.ap(), in_=den[0:1, :]), "st", 16)
                ve_cp = em.val("ve")
                em.wait("sync", "ve", ve_cp)
                em.do("sync", lambda g: g.dma_start(
                    out=m3o.ap(), in_=mall[:, 0:NTI, :]), "st", 16)
                em.do("sync", lambda g: g.dma_start(
                    out=p3o.ap(), in_=l3[:, 0:NTI, :]), "st", 16)
                if DEBUG:
                    em.do("sync", lambda g: g.dma_start(
                        out=dbg_pall.ap(), in_=pall[:, :, :]), "st", 16)
                    em.do("sync", lambda g: g.dma_start(
                        out=dbg_ssb.ap(), in_=ssb[:, :, :]), "st", 16)
            else:
                ld_cp = em.do("sync", lambda g: g.dma_start(
                    out=mall[:, :, :], in_=m3f.ap()), "ld", 16)
                ld_cp = em.do("sync", lambda g: g.dma_start(
                    out=pall[:, 0:NTI, :], in_=p3.ap()), "ld", 16)
                em.wait("vector", "ld", ld_cp)

                def ln_load_stepper():
                    state = {"t": 0}

                    def step():
                        t = state["t"]
                        if t >= NTI:
                            return False
                        em.do("sync", lambda g, tt=t: g.dma_start(
                            out=xnall[:, tt, :],
                            in_=x2nI.ap()[tt * 128:(tt + 1) * 128, :]),
                            "ld", 16)
                        state["t"] = t + 1
                        return True

                    return step

                ln_stepB = ln_load_stepper()
                radix_search(em, filler=ln_stepB)
                keep_all(em, with_probs=False)
                extract_own(em, mul_p3=True)
                while ln_stepB():
                    pass
                em.barrier()
                if SPARSE:
                    ffn_sparse(em, x_in, x_out, zf_st)
                else:
                    ffn(em, x_in, x_out)
            em.wait("sync", "st")
            em.wait("sync", "ld")
            em.wait("sync", "lg")

        @block.sync
        def _(eng):
            program(Emit(sems, "sync", eng, nc))

        @block.gpsimd
        def _(eng):
            program(Emit(sems, "gpsimd", eng, nc))

        @block.tensor
        def _(eng):
            program(Emit(sems, "tensor", eng, nc))

        @block.scalar
        def _(eng):
            program(Emit(sems, "scalar", eng, nc))

        @block.vector
        def _(eng):
            program(Emit(sems, "vector", eng, nc))

    return nc


def kernel(L0_attn_x, L0_attn_Wq, L0_attn_Wk, L0_attn_Wv, L0_attn_Wo,
           L0_attn_rel_emb, L0_ffn_gate_w, L0_ffn_w1, L0_ffn_w2):
    B, T, _ = L0_attn_x.shape
    assert B == NCORE
    x = np.ascontiguousarray(np.asarray(L0_attn_x, dtype=np.float32))
    x0T = np.ascontiguousarray(x.reshape(B * T, D).T)
    gwo_a = np.zeros((D, 16), dtype=np.float32)
    gwo_a[:, 0:E] = np.asarray(L0_ffn_gate_w, dtype=np.float32).T
    gwo_a[:, E] = 1.0
    w1t_a = np.ascontiguousarray(
        np.asarray(L0_ffn_w1, dtype=np.float32).transpose(0, 2, 1))
    w2t_a = np.ascontiguousarray(
        np.asarray(L0_ffn_w2, dtype=np.float32).transpose(0, 2, 1))
    wq_a = np.ascontiguousarray(np.asarray(L0_attn_Wq, dtype=np.float32))
    wk_a = np.ascontiguousarray(np.asarray(L0_attn_Wk, dtype=np.float32))
    wv_a = np.ascontiguousarray(np.asarray(L0_attn_Wv, dtype=np.float32))
    wo_a = np.ascontiguousarray(np.asarray(L0_attn_Wo, dtype=np.float32))
    rel_a = np.ascontiguousarray(np.asarray(L0_attn_rel_emb, dtype=np.float32))

    ncA = build_program(T, "A")
    in_maps = [{
        "x0": np.ascontiguousarray(x[c]),
        "x0T": x0T,
        "wq": wq_a, "wk": wk_a, "wv": wv_a, "wo": wo_a, "rel": rel_a,
        "gwo": gwo_a, "w1t": w1t_a, "w2t": w2t_a,
    } for c in range(NCORE)]
    resA = run_bass_kernel_spmd(ncA, in_maps, core_ids=list(range(NCORE)))
    x2s = [np.asarray(r["x2o"]) for r in resA.results]
    x2ns = [np.asarray(r["x2n"]) for r in resA.results]
    m3s = [np.asarray(r["m3o"]) for r in resA.results]
    p3s = [np.asarray(r["p3o"]) for r in resA.results]
    m3f = np.ascontiguousarray(np.concatenate(m3s, axis=1))

    ncB = build_program(T, "B")
    in_maps = [{
        "x2": np.ascontiguousarray(x2s[c]),
        "x2n": np.ascontiguousarray(x2ns[c]),
        "m3f": m3f,
        "p3": np.ascontiguousarray(p3s[c]),
        "gwo": gwo_a, "w1t": w1t_a, "w2t": w2t_a,
    } for c in range(NCORE)]
    resB = run_bass_kernel_spmd(ncB, in_maps, core_ids=list(range(NCORE)))
    out = np.stack([np.asarray(r["x3o"]) for r in resB.results], axis=0)
    return out.astype(np.float32)
